# version 03
# baseline (speedup 1.0000x reference)
import numpy as np
from contextlib import ExitStack

import concourse.bass as bass
import concourse.mybir as mybir
from concourse.bass_utils import run_bass_kernel_spmd

F32 = mybir.dt.float32
BF16 = mybir.dt.bfloat16
AF = mybir.ActivationFunctionType
ALU = mybir.AluOpType
AX = mybir.AxisListType

B = 8
T = 2048
D = 1024
DI = 2048
NST = 16
RK = 64
NDB = DI // 128
NMB = D // 128
NE = 8
FF = 2048
NFB = FF // 128
EPS = 1e-5
TQ = 512
NQ = T // TQ
NTB = TQ // 128

ALL = ("*ALL*",)
DBG = None



class Sched:

    ENGINES = ("tensor", "vector", "scalar", "gpsimd", "sync")

    def __init__(self):
        self.ops = []
        self.regions = {}

    def add(self, engine, emit, reads=(), writes=(), dma=False, tag=""):
        idx = len(self.ops)
        op = dict(i=idx, eng=engine, emit=emit, dma=dma, tag=tag, deps=set())
        reads = list(reads)
        writes = list(writes)

        def conflicts(name, sub):
            d = self.regions.get(name)
            if d is None:
                return []
            if sub is ALL:
                return list(d.keys())
            out = []
            if sub in d:
                out.append(sub)
            if ALL in d:
                out.append(ALL)
            return out

        for (name, sub) in reads:
            for k in conflicts(name, sub):
                w, _ = self.regions[name][k]
                if w is not None:
                    op["deps"].add(w["i"])
        for (name, sub) in writes:
            for k in conflicts(name, sub):
                w, rs = self.regions[name][k]
                if w is not None:
                    op["deps"].add(w["i"])
                for r in rs:
                    op["deps"].add(r["i"])
        for (name, sub) in reads:
            d = self.regions.setdefault(name, {})
            if sub not in d:
                d[sub] = (None, [])
            d[sub][1].append(op)
        for (name, sub) in writes:
            d = self.regions.setdefault(name, {})
            if sub is ALL:
                for k in list(d.keys()):
                    d[k] = (op, [])
                d[ALL] = (op, [])
            else:
                d[sub] = (op, [])
        self.ops.append(op)
        return op

    def finalize(self, block, sems, dma_sems):
        ops = self.ops
        pos = {e: 0 for e in self.ENGINES}
        for op in ops:
            op["pos"] = pos[op["eng"]]
            pos[op["eng"]] += 1
        for op in ops:
            op["deps"] = {d for d in op["deps"]
                          if ops[d]["dma"] or ops[d]["eng"] != op["eng"]
                          or (op["pos"] - ops[d]["pos"]) <= 6}
        waited = set()
        for op in ops:
            waited.update(op["deps"])
        cnt = {e: 0 for e in self.ENGINES}
        dcnt = {e: 0 for e in self.ENGINES}
        for op in ops:
            if op["dma"]:
                dcnt[op["eng"]] += 1
                op["count"] = dcnt[op["eng"]] * 16
                op["inc"] = True
            else:
                op["inc"] = op["i"] in waited
                if op["inc"]:
                    cnt[op["eng"]] += 1
                op["count"] = cnt[op["eng"]]

        def make_body(eng_name):
            mine = [op for op in ops if op["eng"] == eng_name]

            def body(eng):
                seen = {}
                for op in mine:
                    needs = {}
                    for d in op["deps"]:
                        dop = ops[d]
                        key = (dop["eng"], dop["dma"])
                        if needs.get(key, 0) < dop["count"]:
                            needs[key] = dop["count"]
                    for key in sorted(needs, key=lambda k: (k[0], k[1])):
                        v = needs[key]
                        if seen.get(key, 0) >= v:
                            continue
                        tgt, is_dma = key
                        eng.wait_ge(dma_sems[tgt] if is_dma else sems[tgt], v)
                        seen[key] = v
                    inst = op["emit"](eng)
                    if op["inc"]:
                        assert inst is not None, f"op {op['tag']} needs inst"
                        if op["dma"]:
                            inst.then_inc(dma_sems[eng_name], 16)
                        else:
                            inst.then_inc(sems[eng_name], 1)
            return body

        for name in self.ENGINES:
            getattr(block, name)(make_body(name))


def build_program():
    nc = bass.Bass("TRN2", target_bir_lowering=False)

    h_x = nc.dram_tensor("x", [T, D], F32, kind="ExternalInput").ap()
    h_win = nc.dram_tensor("w_in", [D, 2 * DI], F32, kind="ExternalInput").ap()
    h_convw = nc.dram_tensor("conv_w", [DI, 1, 4], F32, kind="ExternalInput").ap()
    h_convb = nc.dram_tensor("conv_b", [DI], F32, kind="ExternalInput").ap()
    h_xpw = nc.dram_tensor("x_proj_w", [DI, RK + 2 * NST], F32,
                           kind="ExternalInput").ap()
    h_dtw = nc.dram_tensor("dt_w", [RK, DI], F32, kind="ExternalInput").ap()
    h_dtb = nc.dram_tensor("dt_b", [DI], F32, kind="ExternalInput").ap()
    h_alog = nc.dram_tensor("a_log", [DI, NST], F32, kind="ExternalInput").ap()
    h_dd = nc.dram_tensor("d_vec", [DI], F32, kind="ExternalInput").ap()
    h_wout = nc.dram_tensor("w_out", [DI, D], F32, kind="ExternalInput").ap()
    h_nw = nc.dram_tensor("norm_w", [D], F32, kind="ExternalInput").ap()
    h_wg = nc.dram_tensor("wg", [D, NE], F32, kind="ExternalInput").ap()
    h_wu = nc.dram_tensor("wu", [NE, D, FF], F32, kind="ExternalInput").ap()
    h_wd = nc.dram_tensor("wd", [NE, FF, D], F32, kind="ExternalInput").ap()
    h_w1 = nc.dram_tensor("w1", [D, FF], F32, kind="ExternalInput").ap()
    h_w2 = nc.dram_tensor("w2", [FF, D], F32, kind="ExternalInput").ap()
    h_iden = nc.dram_tensor("c_iden", [128, 128], BF16, kind="ExternalInput").ap()
    h_esel = nc.dram_tensor("c_esel", [NST, NST * 128], BF16,
                            kind="ExternalInput").ap()
    h_out = nc.dram_tensor("out", [T, D], F32, kind="ExternalOutput").ap()
    h_dbg = (nc.dram_tensor("dbg", [128, 16384], F32, kind="ExternalOutput").ap()
             if DBG is not None else None)

    with ExitStack() as ctx:
        def sb(name, shape, dt):
            return ctx.enter_context(nc.sbuf_tensor(name, shape, dt))

        def psum_t(name, shape, dt):
            return ctx.enter_context(nc.psum_tensor(name, shape, dt))

        x0 = sb("x0", [128, NTB, D], F32)
        x1 = sb("x1", [128, NTB, D], BF16)
        x2 = sb("x2", [128, NTB, D], F32)
        nt_wi = sb("nt_wi", [128, 4096], BF16)
        n_tok = nt_wi[:].rearrange("p (tb d) -> p tb d", tb=NTB)
        w_in = nt_wi[:].rearrange("p (s kb c) -> p s kb c", s=2, kb=NMB)
        n_cm = sb("n_cm", [128, NMB, TQ], BF16)
        xbig = sb("xbig", [128, NDB * (TQ + 4)], BF16)
        carry = sb("carry", [128, NDB, NST], F32)
        halo = sb("halo", [128, NDB, 4], BF16)
        w_dt = sb("w_dt_sb", [RK, NDB, 128], BF16)
        w_xp = sb("w_xp_sb", [128, NDB, RK + 2 * NST], BF16)
        w_g = sb("w_g_sb", [128, NMB, NE], BF16)
        cwv = sb("conv_w_sb", [128, NDB, 4], F32)
        cbv = sb("conv_b_sb", [128, NDB], F32)
        dtbv = sb("dt_b_sb", [128, NDB], F32)
        ddv = sb("d_vec_sb", [128, NDB], F32)
        a_co = sb("a_co", [128, NDB, NST], F32)
        nwb = sb("nwb", [128, D], BF16)
        iden = sb("iden_sb", [128, 128], BF16)
        esel = sb("esel_sb", [NST, NST * 128], BF16)
        scr = sb("scr", [128, 2, TQ], F32)
        du = sb("du", [128, TQ], BF16)
        sdu = sb("sdu", [128, TQ], F32)
        bfixb = sb("bfixb", [128, 8], F32)
        cst = sb("cst", [128, 2], F32)
        rows = sb("rows", [128, 112], F32)

        AR = sb("AR", [128, 57344], BF16)
        arv = AR[:]

        def arena(off_kb, n_elts):
            o = off_kb * 512
            assert o + n_elts <= 57344
            return arv[:, o:o + n_elts]

        sr = arena(0, 8192).rearrange("p (db t) -> p db t", db=NDB)
        xi2 = arena(16, 8192).rearrange("p (db t) -> p db t", db=NDB)
        dly = arena(32, 8192).rearrange("p (db t) -> p db t", db=NDB)
        dA = arena(48, 8192).bitcast(F32).rearrange(
            "p (r g t) -> p r g t", r=2, g=4)
        dBu = arena(64, 4096).rearrange("p (r g t) -> p r g t", r=2, g=4)
        states = arena(72, 4096).rearrange("p (n t) -> p n t", n=8)
        bbuf = arena(80, 8192).rearrange("p (n t) -> p n t", n=NST)
        cbuf = arena(96, 8192).rearrange("p (n t) -> p n t", n=NST)
        w_out = n_cm[:].rearrange("p a b -> p (a b)").rearrange(
            "p (kb d) -> p kb d", kb=4)
        nwrow = arena(80, D)[0:1, :]
        alog_v = arena(84, 512).bitcast(F32).rearrange(
            "p (db n) -> p db n", db=NDB)
        w_u = arena(0, 16384).rearrange("p (s kb f) -> p s kb f", s=2, kb=NMB)
        w_d = arena(32, 8192).rearrange("p (kb d) -> p kb d", kb=8)
        h2 = arena(48, 8192).rearrange("p (fb t) -> p fb t", fb=NFB)
        acc = arena(64, 8192).bitcast(F32).rearrange(
            "p (tb d) -> p tb d", tb=NTB)
        cwb = arena(80, 4096).rearrange("p (e t) -> p e t", e=NE)
        dlow = arena(72, 512)[0:RK, :]
        brow = arena(73, 512)[0:NST, :]
        crow = arena(74, 512)[0:NST, :]
        cwpad = arena(88, 128)
        cwT = arena(89, 512)[0:NE, :]
        cw = arena(90, 64).bitcast(F32).rearrange("p (tb e) -> p tb e", tb=NTB)
        grow = arena(90, 256)[:, 64:128].bitcast(F32).rearrange(
            "p (tb e) -> p tb e", tb=NTB)
        et = arena(90, 384)[:, 128:192].bitcast(F32).rearrange(
            "p (tb e) -> p tb e", tb=NTB)
        gmask = arena(90, 512)[:, 192:256].bitcast(F32).rearrange(
            "p (tb e) -> p tb e", tb=NTB)

        xi = xbig[:].rearrange("p (db t) -> p db t", db=NDB)
        ao_f = xbig[:].bitcast(F32)[:, 0:NTB * D].rearrange(
            "p (tb d) -> p tb d", tb=NTB)
        x4_f = ao_f

        pb = [psum_t(f"pb{i}", [128, 512], F32) for i in range(4)]
        pbig0 = psum_t("pbig0", [128, 1024], F32)
        pbig1 = psum_t("pbig1", [128, 1024], F32)
        pbt = [pb[0][:].bitcast(BF16), pb[1][:].bitcast(BF16)]

        sems = {}
        dma_sems = {}
        for e in Sched.ENGINES:
            sems[e] = ctx.enter_context(nc.semaphore(f"s_{e}"))
            dma_sems[e] = ctx.enter_context(nc.semaphore(f"d_{e}"))
        block = ctx.enter_context(nc.Block())
        S = Sched()

        def dma(queue, out_ap, in_ap, reads, writes, tag=""):
            S.add(queue, lambda g, o=out_ap, i=in_ap: g.dma_start(out=o, in_=i),
                  reads=reads, writes=writes, dma=True, tag=tag)

        def tap(name, ap2d, reads):
            if DBG == name:
                n = ap2d.free_size()
                dma("gpsimd", h_dbg[:, 0:n], ap2d, reads,
                    [("dbgout", 0)], f"tap.{name}")

        def V(emit, reads, writes, tag=""):
            return S.add("vector", emit, reads, writes, tag=tag)

        def A(emit, reads, writes, tag=""):
            return S.add("scalar", emit, reads, writes, tag=tag)

        def TE(emit, reads, writes, tag=""):
            return S.add("tensor", emit, reads, writes, tag=tag)

        def fence_ar(tag):
            V(lambda v: v.memset(arv[:, 0:1], 0.0), (), [("AR", ALL)], tag)

        def fence_xbig(tag):
            V(lambda v: v.memset(xbig[:, 0:1], 0.0), (), [("xbig", ALL)], tag)

        def fence_ntw(tag):
            V(lambda v: v.memset(nt_wi[:, 0:1], 0.0), (),
              [("n_tok", ALL), ("w_in", ALL)], tag)

        def K_AR(*k):
            return ("AR", k)

        def K(name, *k):
            return (name, k)

        dma("sync", iden[:], h_iden, (), [K("iden")], "iden")
        dma("sync", esel[:], h_esel, (), [K("esel")], "esel")
        dma("gpsimd", nwrow, h_nw.rearrange("(o d) -> o d", o=1),
            (), [K_AR("pro_nw")], "nw")
        dma("gpsimd", w_dt[:], h_dtw.rearrange("r (jb c) -> r jb c", c=128),
            (), [K("w_dt")], "dtw")
        for kb in range(NDB):
            dma("gpsimd", w_xp[:, kb, :], h_xpw[kb * 128:(kb + 1) * 128, :],
                (), [K("w_xp", kb)], f"xpw{kb}")
        for kb in range(NMB):
            dma("gpsimd", w_g[:, kb, :], h_wg[kb * 128:(kb + 1) * 128, :],
                (), [K("w_g", kb)], f"wg{kb}")
        for db in range(NDB):
            sl = slice(db * 128, (db + 1) * 128)
            dma("sync", cwv[:, db, :], h_convw[sl, 0, :], (), [K("cwv", db)])
            dma("sync", cbv[:, db:db + 1],
                h_convb[sl].rearrange("(p o) -> p o", o=1), (), [K("cbv", db)])
            dma("sync", dtbv[:, db:db + 1],
                h_dtb[sl].rearrange("(p o) -> p o", o=1), (), [K("dtbv", db)])
            dma("sync", ddv[:, db:db + 1],
                h_dd[sl].rearrange("(p o) -> p o", o=1), (), [K("ddv", db)])
            dma("sync", alog_v[:, db, :], h_alog[sl, :], (),
                [K_AR("pro_alog", db)], f"alog{db}")
        A(lambda s: s.activation(out=a_co[:], in_=alog_v[:, :, :], func=AF.Exp),
          [K_AR("pro_alog", db) for db in range(NDB)], [K("a_co")], "A_exp")
        V(lambda v: v.tensor_scalar(out=a_co[:], in0=a_co[:], scalar1=-1.0,
                                    scalar2=None, op0=ALU.mult),
          [K("a_co")], [K("a_co")], "A_neg")
        for half in range(2):
            TE(lambda t, half=half: t.matmul(
                pbig0[:, half * 512:(half + 1) * 512], esel[0:1, 0:128],
                nwrow[:, half * 512:(half + 1) * 512], start=True, stop=True),
               [K("esel"), K_AR("pro_nw")], [K("pbig0")], f"nwb_mm{half}")
        A(lambda s: s.activation(out=nwb[:], in_=pbig0[:, 0:D], func=AF.Copy),
          [K("pbig0")], [K("nwb")], "nwb_ev")
        V(lambda v: v.memset(carry[:], 0.0), (),
          [K("carry", db) for db in range(NDB)], "carry0")
        V(lambda v: v.memset(cst[:, 0:1], EPS), (), [K("cst")], "cst0")
        V(lambda v: v.memset(cst[:, 1:2], 1.0), (), [K("cst")], "cst1")
        V(lambda v: v.memset(cwpad[:], 0.0), (), [K_AR("cwpad")], "cwpad0")

        def ln_stats(src_ap, key_of_tb, rc, tag):
            for tb in range(NTB):
                A(lambda s, tb=tb: s.activation(
                    out=scr[:].rearrange("p a b -> p (a b)")[:, 0:D],
                    in_=src_ap[:, tb, :], func=AF.Square,
                    accum_out=rows[:, rc + 8 + tb:rc + 9 + tb]),
                  [key_of_tb(tb)], [K("scr", 0), K("scr", 1),
                                    K("rows", rc + 8 + tb)], f"{tag}.sq{tb}")
                V(lambda v, tb=tb: v.tensor_reduce(
                    out=rows[:, rc + 12 + tb:rc + 13 + tb],
                    in_=src_ap[:, tb, :], op=ALU.add, axis=AX.X),
                  [key_of_tb(tb)], [K("rows", rc + 12 + tb)], f"{tag}.s{tb}")
            V(lambda v: v.tensor_scalar(
                out=rows[:, rc + 4:rc + 8], in0=rows[:, rc + 12:rc + 16],
                scalar1=1.0 / D, scalar2=None, op0=ALU.mult),
              [K("rows", rc + 12 + t) for t in range(4)],
              [K("rows", rc + 4 + t) for t in range(4)], f"{tag}.m")
            V(lambda v: v.scalar_tensor_tensor(
                out=rows[:, rc + 16:rc + 20], in0=rows[:, rc + 4:rc + 8],
                scalar=-1.0, in1=rows[:, rc + 4:rc + 8],
                op0=ALU.mult, op1=ALU.mult),
              [K("rows", rc + 4 + t) for t in range(4)],
              [K("rows", rc + 16 + t) for t in range(4)], f"{tag}.mm")
            V(lambda v: v.scalar_tensor_tensor(
                out=rows[:, rc + 16:rc + 20], in0=rows[:, rc + 8:rc + 12],
                scalar=1.0 / D, in1=rows[:, rc + 16:rc + 20],
                op0=ALU.mult, op1=ALU.add),
              [K("rows", rc + 8 + t) for t in range(4)] +
              [K("rows", rc + 16 + t) for t in range(4)],
              [K("rows", rc + 16 + t) for t in range(4)], f"{tag}.var")
            A(lambda s: s.activation(
                out=rows[:, rc + 16:rc + 20], in_=rows[:, rc + 16:rc + 20],
                func=AF.Ln, bias=cst[:, 0:1]),
              [K("rows", rc + 16 + t) for t in range(4)] + [K("cst")],
              [K("rows", rc + 16 + t) for t in range(4)], f"{tag}.ln")
            A(lambda s: s.activation(
                out=rows[:, rc:rc + 4], in_=rows[:, rc + 16:rc + 20],
                func=AF.Exp, scale=-0.5),
              [K("rows", rc + 16 + t) for t in range(4)],
              [K("rows", rc + t) for t in range(4)], f"{tag}.rstd")
            V(lambda v: v.scalar_tensor_tensor(
                out=rows[:, rc + 4:rc + 8], in0=rows[:, rc + 4:rc + 8],
                scalar=-1.0, in1=rows[:, rc:rc + 4],
                op0=ALU.mult, op1=ALU.mult),
              [K("rows", rc + t) for t in range(8)],
              [K("rows", rc + 4 + t) for t in range(4)], f"{tag}.nmr")

        def transpose_tok_to_cm(tag):
            for mb in range(NMB):
                bank = mb % 2
                pkey = K(f"pb{bank}")
                for tb in range(NTB):
                    TE(lambda t, mb=mb, tb=tb, bank=bank: t.transpose(
                        pbt[bank][:, tb * 128:(tb + 1) * 128],
                        n_tok[:, tb, mb * 128:(mb + 1) * 128], iden[:]),
                       [K("n_tok", tb), K("iden")], [pkey],
                       f"{tag}.t{mb}.{tb}")
                A(lambda s, mb=mb, bank=bank: s.activation(
                    out=n_cm[:, mb, :], in_=pbt[bank][:, 0:TQ], func=AF.Copy),
                  [pkey], [K("n_cm", mb)], f"{tag}.ev{mb}")

        for q in range(NQ):
            q0 = q * TQ

            for tb in range(NTB):
                dma("sync", x0[:, tb, :],
                    h_x[q0 + tb * 128:q0 + (tb + 1) * 128, :],
                    (), [K("x0", tb)], f"x0.{q}.{tb}")

            ln_stats(x0, lambda tb: K("x0", tb), 0, f"ln1.{q}")
            for tb in range(NTB):
                V(lambda v, tb=tb: v.tensor_scalar(
                    out=x1[:, tb, :], in0=x0[:, tb, :],
                    scalar1=rows[:, tb:tb + 1], scalar2=rows[:, 4 + tb:5 + tb],
                    op0=ALU.mult, op1=ALU.add),
                  [K("x0", tb), K("rows", tb), K("rows", 4 + tb)],
                  [K("x1", tb)], f"ln1.ap{q}.{tb}")

            for tb in range(NTB):
                A(lambda s, tb=tb: s.activation(
                    out=scr[:].rearrange("p a b -> p (a b)")[:, 0:D],
                    in_=x1[:, tb, :], func=AF.Square,
                    accum_out=rows[:, 20 + tb:21 + tb]),
                  [K("x1", tb)], [K("scr", 0), K("scr", 1), K("rows", 20 + tb)],
                  f"rms{q}.sq{tb}")
            A(lambda s: s.activation(out=rows[:, 24:28], in_=rows[:, 20:24],
                                     func=AF.Ln, scale=1.0 / D,
                                     bias=cst[:, 0:1]),
              [K("rows", 20 + t) for t in range(4)] + [K("cst")],
              [K("rows", 24 + t) for t in range(4)], f"rms{q}.ln")
            A(lambda s: s.activation(out=rows[:, 24:28], in_=rows[:, 24:28],
                                     func=AF.Exp, scale=-0.5),
              [K("rows", 24 + t) for t in range(4)],
              [K("rows", 24 + t) for t in range(4)], f"rms{q}.rinv")
            for tb in range(NTB):
                V(lambda v, tb=tb: v.tensor_scalar(
                    out=n_tok[:, tb, :], in0=x1[:, tb, :],
                    scalar1=rows[:, 24 + tb:25 + tb], scalar2=None,
                    op0=ALU.mult),
                  [K("x1", tb), K("rows", 24 + tb)], [K("n_tok", tb)],
                  f"rms{q}.m{tb}")
                V(lambda v, tb=tb: v.tensor_tensor(
                    out=n_tok[:, tb, :], in0=n_tok[:, tb, :], in1=nwb[:],
                    op=ALU.mult),
                  [K("n_tok", tb), K("nwb")], [K("n_tok", tb)], f"rms{q}.w{tb}")

            if q == 0:
                tap("rows1", rows[:, 0:20],
                    [K("rows", c) for c in range(20)])
                tap("x0t", x0[:].rearrange("p a b -> p (a b)"),
                    [K("x0", tb) for tb in range(NTB)])
                tap("x1", x1[:].rearrange("p a b -> p (a b)"),
                    [K("x1", tb) for tb in range(NTB)])
                tap("ntok", n_tok.rearrange("p a b -> p (a b)"),
                    [K("n_tok", tb) for tb in range(NTB)])
            transpose_tok_to_cm(f"tx1.{q}")
            if q == 0:
                tap("ncm", n_cm[:].rearrange("p a b -> p (a b)"),
                    [K("n_cm", mb) for mb in range(NMB)])

            fence_ar(f"f_mamba{q}")
            fence_xbig(f"f_xi{q}")
            fence_ntw(f"f_win{q}")

            for cg in range(16):
                slot = cg % 2
                for hh in range(2):
                    src = h_win.rearrange("(kb p) n -> p kb n", p=128)[
                        :, hh * 4:(hh + 1) * 4, cg * 256:(cg + 1) * 256]
                    dma("gpsimd", w_in[:, slot, hh * 4:(hh + 1) * 4, :], src,
                        (), [K("w_in", slot)], f"win{q}.{cg}.{hh}")
                for j in range(2):
                    jb = cg * 2 + j
                    bank = pb[2 + (jb % 2)]
                    pkey = K(f"pb{2 + (jb % 2)}")
                    for kb in range(NMB):
                        TE(lambda t, slot=slot, j=j, kb=kb, bank=bank: t.matmul(
                            bank[:], w_in[:, slot, kb, j * 128:(j + 1) * 128],
                            n_cm[:, kb, :], start=(kb == 0),
                            stop=(kb == NMB - 1)),
                           [K("w_in", slot), K("n_cm", kb)], [pkey],
                           f"ip{q}.{jb}.{kb}")
                    if jb < 16:
                        db = jb
                        A(lambda s, db=db, bank=bank: s.activation(
                            out=xi[:, db, 4:4 + TQ], in_=bank[:], func=AF.Copy),
                          [pkey], [K("xbig", "xi", db)], f"ipev{q}.{db}")
                    else:
                        db = jb - 16
                        A(lambda s, db=db, bank=bank: s.activation(
                            out=sr[:, db, :], in_=bank[:], func=AF.Silu),
                          [pkey], [K_AR("sr", db)], f"srev{q}.{db}")

            if q == 0:
                tap("xi", xi.rearrange("p a b -> p (a b)"),
                    [K("xbig", "xi", db) for db in range(NDB)])
                tap("sr", sr.rearrange("p a b -> p (a b)"),
                    [K_AR("sr", db) for db in range(NDB)])
            fence_ntw(f"f_ntok{q}")

            def wo_dma(kb, q=q):
                s_ = kb % 4
                dma("gpsimd", w_out[:, s_, :],
                    h_wout[kb * 128:(kb + 1) * 128, :],
                    (), [K("n_cm", 2 * s_), K("n_cm", 2 * s_ + 1)],
                    f"wo{q}.{kb}")

            for kb in range(4):
                wo_dma(kb)

            for db in range(NDB):
                if q > 0:
                    V(lambda v, db=db: v.tensor_copy(xi[:, db, 0:4],
                                                     halo[:, db, :]),
                      [K("halo", db)], [K("xbig", "xi", db)], f"hin{q}.{db}")
                else:
                    V(lambda v, db=db: v.memset(xi[:, db, 0:4], 0.0),
                      (), [K("xbig", "xi", db)], f"hz{q}.{db}")
                V(lambda v, db=db: v.tensor_scalar(
                    out=scr[:, 0, :], in0=xi[:, db, 1:1 + TQ],
                    scalar1=cwv[:, db, 0:1], scalar2=None, op0=ALU.mult),
                  [K("xbig", "xi", db), K("cwv", db)], [K("scr", 0)],
                  f"cv0.{q}.{db}")
                for k in (1, 2, 3):
                    V(lambda v, db=db, k=k: v.scalar_tensor_tensor(
                        out=scr[:, 0, :], in0=xi[:, db, 1 + k:1 + k + TQ],
                        scalar=cwv[:, db, k:k + 1], in1=scr[:, 0, :],
                        op0=ALU.mult, op1=ALU.add),
                      [K("xbig", "xi", db), K("cwv", db), K("scr", 0)],
                      [K("scr", 0)], f"cv{k}.{q}.{db}")
                V(lambda v, db=db: v.tensor_copy(halo[:, db, :],
                                                 xi[:, db, TQ:TQ + 4]),
                  [K("xbig", "xi", db)], [K("halo", db)], f"hsv{q}.{db}")
                A(lambda s, db=db: s.activation(
                    out=xi2[:, db, :], in_=scr[:, 0, :], func=AF.Silu,
                    bias=cbv[:, db:db + 1]),
                  [K("scr", 0), K("cbv", db)], [K_AR("xi2", db)],
                  f"cvs{q}.{db}")

            if q == 0:
                tap("xi2", xi2.rearrange("p a b -> p (a b)"),
                    [K_AR("xi2", db) for db in range(NDB)])

            for (c0, nco, p0, bank, pkey, dkey, dst) in (
                    (0, RK, 0, pb[2], K("pb2"), K_AR("st", 0), dlow),
                    (RK, NST, 64, pb[2], K("pb2"), K_AR("st", 1), brow),
                    (RK + NST, NST, 0, pb[3], K("pb3"), K_AR("st", 2), crow)):
                for kb in range(NDB):
                    TE(lambda t, kb=kb, c0=c0, nco=nco, p0=p0, bank=bank:
                       t.matmul(bank[p0:p0 + nco, :], w_xp[:, kb, c0:c0 + nco],
                                xi2[:, kb, :], start=(kb == 0),
                                stop=(kb == NDB - 1)),
                       [K("w_xp", kb), K_AR("xi2", kb)], [pkey],
                       f"xp{q}.{c0}.{kb}")
                A(lambda s, p0=p0, nco=nco, bank=bank, dst=dst: s.activation(
                    out=dst[:, :], in_=bank[p0:p0 + nco, :], func=AF.Copy),
                  [pkey], [dkey], f"xpev{q}.{c0}")

            for db in range(NDB):
                bank = pb[db % 2]
                pkey = K(f"pb{db % 2}")
                TE(lambda t, db=db, bank=bank: t.matmul(
                    bank[:], w_dt[:, db, :], dlow[:, :], start=True, stop=True),
                   [K("w_dt"), K_AR("st", 0)], [pkey], f"dt{q}.{db}")
                A(lambda s, db=db, bank=bank: s.activation(
                    out=scr[:, 0, :], in_=bank[:], func=AF.Relu,
                    bias=dtbv[:, db:db + 1]),
                  [pkey, K("dtbv", db)], [K("scr", 0)], f"dtr{q}.{db}")
                A(lambda s, db=db, bank=bank: s.activation(
                    out=scr[:, 1, :], in_=bank[:], func=AF.Abs,
                    bias=dtbv[:, db:db + 1]),
                  [pkey, K("dtbv", db)], [K("scr", 1)], f"dta{q}.{db}")
                A(lambda s: s.activation(out=scr[:, 1, :], in_=scr[:, 1, :],
                                         func=AF.Exp, scale=-1.0),
                  [K("scr", 1)], [K("scr", 1)], f"dte{q}.{db}")
                A(lambda s: s.activation(out=scr[:, 1, :], in_=scr[:, 1, :],
                                         func=AF.Ln, bias=cst[:, 1:2]),
                  [K("scr", 1), K("cst")], [K("scr", 1)], f"dtl{q}.{db}")
                V(lambda v, db=db: v.tensor_tensor(
                    out=dly[:, db, :], in0=scr[:, 0, :], in1=scr[:, 1, :],
                    op=ALU.add),
                  [K("scr", 0), K("scr", 1)], [K_AR("dly", db)], f"dtd{q}.{db}")

            if q == 0:
                tap("dlow", dlow[:, :], [K_AR("st", 0)])
                tap("brow", brow[:, :], [K_AR("st", 1)])
                tap("crow", crow[:, :], [K_AR("st", 2)])
                tap("delta", dly.rearrange("p a b -> p (a b)"),
                    [K_AR("dly", db) for db in range(NDB)])

            for (srw, skey, dbuf, dname) in ((brow, K_AR("st", 1), bbuf, "bbuf"),
                                             (crow, K_AR("st", 2), cbuf, "cbuf")):
                for n in range(NST):
                    bank = pb[2 + (n % 2)]
                    pkey = K(f"pb{2 + (n % 2)}")
                    TE(lambda t, n=n, bank=bank, srw=srw: t.matmul(
                        bank[:], esel[:, n * 128:(n + 1) * 128], srw[:, :],
                        start=True, stop=True),
                       [K("esel"), skey], [pkey], f"bc{dname}{q}.{n}")
                    A(lambda s, n=n, bank=bank, dbuf=dbuf: s.activation(
                        out=dbuf[:, n, :], in_=bank[:], func=AF.Copy),
                      [pkey], [K_AR(dname, n)], f"bcv{dname}{q}.{n}")

            if q == 0:
                tap("bbuf", bbuf.rearrange("p a b -> p (a b)"),
                    [K_AR("bbuf", n) for n in range(NST)])
                tap("cbuf", cbuf.rearrange("p a b -> p (a b)"),
                    [K_AR("cbuf", n) for n in range(NST)])

            for db in range(NDB):
                V(lambda v, db=db: v.tensor_tensor(
                    out=du[:], in0=dly[:, db, :], in1=xi2[:, db, :],
                    op=ALU.mult),
                  [K_AR("dly", db), K_AR("xi2", db)], [K("du")], f"du{q}.{db}")
                V(lambda v, db=db: v.tensor_scalar(
                    out=sdu[:], in0=xi2[:, db, :], scalar1=ddv[:, db:db + 1],
                    scalar2=None, op0=ALU.mult),
                  [K_AR("xi2", db), K("ddv", db)], [K("sdu")], f"yd{q}.{db}")
                for g in range(4):
                    r = g % 2
                    kdA = [K_AR("dA", r, i) for i in range(4)]
                    kdB = [K_AR("dBu", r, i) for i in range(4)]
                    kst = [K_AR("st", r * 4 + i) for i in range(4)]
                    for i in range(4):
                        n = g * 4 + i
                        A(lambda s, db=db, r=r, i=i, n=n: s.activation(
                            out=dA[:, r, i, :], in_=dly[:, db, :], func=AF.Exp,
                            scale=a_co[:, db, n:n + 1]),
                          [K_AR("dly", db), K("a_co")], [kdA[i]],
                          f"dae{q}.{db}.{n}")
                    V(lambda v, r=r, g=g: v.tensor_tensor(
                        out=dBu[:, r],
                        in0=du[:].unsqueeze(1).broadcast_to([128, 4, TQ]),
                        in1=bbuf[:, g * 4:(g + 1) * 4, :], op=ALU.mult),
                      [K("du")] + [K_AR("bbuf", g * 4 + i) for i in range(4)],
                      kdB, f"dbu{q}.{db}.{g}")
                    V(lambda v, db=db, r=r, g=g: v.tensor_tensor(
                        out=bfixb[:, 0:4].unsqueeze(2), in0=dA[:, r, :, 0:1],
                        in1=carry[:, db, g * 4:(g + 1) * 4].unsqueeze(2),
                        op=ALU.mult),
                      kdA + [K("carry", db)], [K("bfixb")], f"bf{q}.{db}.{g}")
                    V(lambda v, r=r: v.tensor_tensor(
                        out=dBu[:, r, :, 0:1], in0=dBu[:, r, :, 0:1],
                        in1=bfixb[:, 0:4].unsqueeze(2), op=ALU.add),
                      [K("bfixb")] + kdB, kdB, f"bfa{q}.{db}.{g}")
                    V(lambda v, r=r: v.memset(dA[:, r, :, 0:1], 0.0),
                      kdA, kdA, f"daz{q}.{db}.{g}")
                    V(lambda v, r=r: v.tensor_tensor_scan(
                        states[:, r * 4:(r + 1) * 4, :].rearrange(
                            "p a b -> p (a b)"),
                        dA[:, r].rearrange("p a b -> p (a b)"),
                        dBu[:, r].rearrange("p a b -> p (a b)"),
                        0.0, ALU.mult, ALU.add),
                      kdA + kdB, kst, f"tts{q}.{db}.{g}")
                    V(lambda v, db=db, r=r, g=g: v.tensor_copy(
                        carry[:, db, g * 4:(g + 1) * 4].unsqueeze(2),
                        states[:, r * 4:(r + 1) * 4, TQ - 1:TQ]),
                      kst, [K("carry", db)], f"csv{q}.{db}.{g}")
                    V(lambda v, r=r, g=g: v.tensor_tensor(
                        out=states[:, r * 4:(r + 1) * 4, :],
                        in0=states[:, r * 4:(r + 1) * 4, :],
                        in1=cbuf[:, g * 4:(g + 1) * 4, :], op=ALU.mult),
                      kst + [K_AR("cbuf", g * 4 + i) for i in range(4)],
                      kst, f"zc{q}.{db}.{g}")
                    V(lambda v, r=r: v.tensor_tensor(
                        out=states[:, r * 4:r * 4 + 2, :],
                        in0=states[:, r * 4:r * 4 + 2, :],
                        in1=states[:, r * 4 + 2:r * 4 + 4, :], op=ALU.add),
                      kst, kst[0:2], f"tr2.{q}.{db}.{g}")
                    V(lambda v, r=r: v.tensor_tensor(
                        out=states[:, r * 4:r * 4 + 1, :],
                        in0=states[:, r * 4:r * 4 + 1, :],
                        in1=states[:, r * 4 + 1:r * 4 + 2, :], op=ALU.add),
                      kst[0:2], kst[0:1], f"tr1.{q}.{db}.{g}")
                    V(lambda v, r=r: v.tensor_tensor(
                        out=sdu[:], in0=sdu[:], in1=states[:, r * 4, :],
                        op=ALU.add),
                      [K("sdu"), kst[0]], [K("sdu")], f"ya{q}.{db}.{g}")
                V(lambda v, db=db: v.tensor_tensor(
                    out=dly[:, db, :], in0=sdu[:], in1=sr[:, db, :],
                    op=ALU.mult),
                  [K("sdu"), K_AR("sr", db)], [K_AR("dly", db)], f"y2{q}.{db}")
                if q == 0 and db == 0:
                    tap("y2db0", dly[:, 0, :], [K_AR("dly", 0)])
                if q == 0 and db == NDB - 1:
                    tap("y2all", dly.rearrange("p a b -> p (a b)"),
                        [K_AR("dly", d) for d in range(NDB)])

            fence_xbig(f"f_ao{q}")

            op_psum = {(0, 0): (pbig0[:, 0:512], K("pbig0")),
                       (0, 1): (pbig0[:, 512:1024], K("pbig0")),
                       (1, 0): (pbig1[:, 0:512], K("pbig1")),
                       (1, 1): (pbig1[:, 512:1024], K("pbig1")),
                       (2, 0): (pb[2][:, 0:512], K("pb2")),
                       (2, 1): (pb[1][:, 0:512], K("pb1")),
                       (3, 0): (pb[3][:, 0:512], K("pb3")),
                       (3, 1): (pb[0][:, 0:512], K("pb0"))}
            for kb in range(NDB):
                for tb in range(NTB):
                    for dmh in range(2):
                        dst, pkey = op_psum[(tb, dmh)]
                        TE(lambda t, kb=kb, tb=tb, dmh=dmh, dst=dst: t.matmul(
                            dst, dly[:, kb, tb * 128:(tb + 1) * 128],
                            w_out[:, kb % 4, dmh * 512:(dmh + 1) * 512],
                            start=(kb == 0), stop=(kb == NDB - 1)),
                           [K_AR("dly", kb), K("n_cm", 2 * (kb % 4)),
                            K("n_cm", 2 * (kb % 4) + 1)], [pkey],
                           f"op{q}.{kb}.{tb}.{dmh}")
                if kb + 4 < NDB:
                    wo_dma(kb + 4)
            for tb in range(NTB):
                for dmh in range(2):
                    src, pkey = op_psum[(tb, dmh)]
                    V(lambda v, tb=tb, dmh=dmh, src=src: v.tensor_tensor(
                        out=ao_f[:, tb, dmh * 512:(dmh + 1) * 512], in0=src,
                        in1=x1[:, tb, dmh * 512:(dmh + 1) * 512], op=ALU.add),
                      [pkey, K("x1", tb)], [K("xbig", "ao", tb)],
                      f"aoev{q}.{tb}.{dmh}")

            if q == 0:
                tap("ao", ao_f.rearrange("p a b -> p (a b)"),
                    [K("xbig", "ao", tb) for tb in range(NTB)])

            ln_stats(ao_f, lambda tb: K("xbig", "ao", tb), 28, f"ln3.{q}")
            for tb in range(NTB):
                V(lambda v, tb=tb: v.tensor_scalar(
                    out=scr[:].rearrange("p a b -> p (a b)")[:, 0:D],
                    in0=ao_f[:, tb, :], scalar1=rows[:, 28 + tb:29 + tb],
                    scalar2=rows[:, 32 + tb:33 + tb],
                    op0=ALU.mult, op1=ALU.add),
                  [K("xbig", "ao", tb), K("rows", 28 + tb),
                   K("rows", 32 + tb)],
                  [K("scr", 0), K("scr", 1)], f"ln3a{q}.{tb}")
                V(lambda v, tb=tb: v.tensor_tensor(
                    out=x2[:, tb, :],
                    in0=scr[:].rearrange("p a b -> p (a b)")[:, 0:D],
                    in1=x0[:, tb, :], op=ALU.add),
                  [K("scr", 0), K("scr", 1), K("x0", tb)], [K("x2", tb)],
                  f"ln3b{q}.{tb}")

            ln_stats(x2, lambda tb: K("x2", tb), 48, f"ln2.{q}")
            for tb in range(NTB):
                V(lambda v, tb=tb: v.tensor_scalar(
                    out=n_tok[:, tb, :], in0=x2[:, tb, :],
                    scalar1=rows[:, 48 + tb:49 + tb],
                    scalar2=rows[:, 52 + tb:53 + tb],
                    op0=ALU.mult, op1=ALU.add),
                  [K("x2", tb), K("rows", 48 + tb), K("rows", 52 + tb)],
                  [K("n_tok", tb)], f"ln2a{q}.{tb}")

            if q == 0:
                tap("x2", x2[:].rearrange("p a b -> p (a b)"),
                    [K("x2", tb) for tb in range(NTB)])
            transpose_tok_to_cm(f"tx2.{q}")

            fence_ar(f"f_moe{q}")

            for tb in range(NTB):
                for kb in range(NMB):
                    TE(lambda t, tb=tb, kb=kb: t.matmul(
                        pb[2][:, 0:NE], n_cm[:, kb, tb * 128:(tb + 1) * 128],
                        w_g[:, kb, :], start=(kb == 0), stop=(kb == NMB - 1)),
                       [K("n_cm", kb), K("w_g", kb)], [K("pb2")],
                       f"g{q}.{tb}.{kb}")
                A(lambda s, tb=tb: s.activation(
                    out=grow[:, tb, :], in_=pb[2][:, 0:NE], func=AF.Copy),
                  [K("pb2")], [K_AR("grow", tb)], f"gev{q}.{tb}")
            rg = 88
            for tb in range(NTB):
                V(lambda v, tb=tb: v.reduce_max(
                    out=rows[:, rg + tb:rg + tb + 1], in_=grow[:, tb, :],
                    axis=AX.X),
                  [K_AR("grow", tb)], [K("rows", rg + tb)], f"gm{q}.{tb}")
                V(lambda v, tb=tb: v.tensor_scalar(
                    out=rows[:, rg + 4 + tb:rg + 5 + tb],
                    in0=rows[:, rg + tb:rg + tb + 1], scalar1=-1.0,
                    scalar2=None, op0=ALU.mult),
                  [K("rows", rg + tb)], [K("rows", rg + 4 + tb)],
                  f"gn{q}.{tb}")
                A(lambda s, tb=tb: s.activation(
                    out=et[:, tb, :], in_=grow[:, tb, :], func=AF.Exp,
                    bias=rows[:, rg + 4 + tb:rg + 5 + tb]),
                  [K_AR("grow", tb), K("rows", rg + 4 + tb)], [K_AR("et", tb)],
                  f"ge{q}.{tb}")
                V(lambda v, tb=tb: v.reduce_max(
                    out=rows[:, rg + 8 + tb:rg + 9 + tb], in_=et[:, tb, :],
                    axis=AX.X),
                  [K_AR("et", tb)], [K("rows", rg + 8 + tb)], f"gm1{q}.{tb}")
                V(lambda v, tb=tb: v.tensor_scalar(
                    out=gmask[:, tb, :], in0=et[:, tb, :],
                    scalar1=rows[:, rg + 8 + tb:rg + 9 + tb], scalar2=None,
                    op0=ALU.is_equal),
                  [K_AR("et", tb), K("rows", rg + 8 + tb)], [K_AR("gmask", tb)],
                  f"gq1{q}.{tb}")
                V(lambda v, tb=tb: v.tensor_scalar(
                    out=grow[:, tb, :], in0=gmask[:, tb, :], scalar1=-1.0,
                    scalar2=1.0, op0=ALU.mult, op1=ALU.add),
                  [K_AR("gmask", tb)], [K_AR("grow", tb)], f"ga{q}.{tb}")
                V(lambda v, tb=tb: v.tensor_tensor(
                    out=grow[:, tb, :], in0=grow[:, tb, :], in1=et[:, tb, :],
                    op=ALU.mult),
                  [K_AR("grow", tb), K_AR("et", tb)], [K_AR("grow", tb)],
                  f"ge2{q}.{tb}")
                V(lambda v, tb=tb: v.reduce_max(
                    out=rows[:, rg + 12 + tb:rg + 13 + tb],
                    in_=grow[:, tb, :], axis=AX.X),
                  [K_AR("grow", tb)], [K("rows", rg + 12 + tb)], f"gm2{q}.{tb}")
                V(lambda v, tb=tb: v.tensor_tensor(
                    out=rows[:, rg + 16 + tb:rg + 17 + tb],
                    in0=rows[:, rg + 8 + tb:rg + 9 + tb],
                    in1=rows[:, rg + 12 + tb:rg + 13 + tb], op=ALU.add),
                  [K("rows", rg + 8 + tb), K("rows", rg + 12 + tb)],
                  [K("rows", rg + 16 + tb)], f"gs{q}.{tb}")
                V(lambda v, tb=tb: v.reciprocal(
                    out=rows[:, rg + 16 + tb:rg + 17 + tb],
                    in_=rows[:, rg + 16 + tb:rg + 17 + tb]),
                  [K("rows", rg + 16 + tb)], [K("rows", rg + 16 + tb)],
                  f"gr{q}.{tb}")
                V(lambda v, tb=tb: v.tensor_tensor(
                    out=rows[:, rg + 20 + tb:rg + 21 + tb],
                    in0=rows[:, rg + 8 + tb:rg + 9 + tb],
                    in1=rows[:, rg + 16 + tb:rg + 17 + tb], op=ALU.mult),
                  [K("rows", rg + 8 + tb), K("rows", rg + 16 + tb)],
                  [K("rows", rg + 20 + tb)], f"gw1{q}.{tb}")
                V(lambda v, tb=tb: v.tensor_scalar(
                    out=cw[:, tb, :], in0=grow[:, tb, :],
                    scalar1=rows[:, rg + 12 + tb:rg + 13 + tb], scalar2=None,
                    op0=ALU.is_equal),
                  [K_AR("grow", tb), K("rows", rg + 12 + tb)], [K_AR("cw", tb)],
                  f"gq2{q}.{tb}")
                V(lambda v, tb=tb: v.tensor_tensor(
                    out=rows[:, rg + 12 + tb:rg + 13 + tb],
                    in0=rows[:, rg + 12 + tb:rg + 13 + tb],
                    in1=rows[:, rg + 16 + tb:rg + 17 + tb], op=ALU.mult),
                  [K("rows", rg + 12 + tb), K("rows", rg + 16 + tb)],
                  [K("rows", rg + 12 + tb)], f"gw2{q}.{tb}")
                V(lambda v, tb=tb: v.tensor_scalar(
                    out=cw[:, tb, :], in0=cw[:, tb, :],
                    scalar1=rows[:, rg + 12 + tb:rg + 13 + tb], scalar2=None,
                    op0=ALU.mult),
                  [K_AR("cw", tb), K("rows", rg + 12 + tb)], [K_AR("cw", tb)],
                  f"gc2{q}.{tb}")
                V(lambda v, tb=tb: v.scalar_tensor_tensor(
                    out=cw[:, tb, :], in0=gmask[:, tb, :],
                    scalar=rows[:, rg + 20 + tb:rg + 21 + tb],
                    in1=cw[:, tb, :], op0=ALU.mult, op1=ALU.add),
                  [K_AR("gmask", tb), K("rows", rg + 20 + tb), K_AR("cw", tb)],
                  [K_AR("cw", tb)], f"gc1{q}.{tb}")

            for tb in range(NTB):
                V(lambda v, tb=tb: v.tensor_copy(cwpad[:, 0:NE], cw[:, tb, :]),
                  [K_AR("cw", tb)], [K_AR("cwpad")], f"cpad{q}.{tb}")
                TE(lambda t, tb=tb: t.transpose(
                    pbt[0][:, 0:128], cwpad[:, :], iden[:]),
                   [K_AR("cwpad"), K("iden")], [K("pb0")], f"ctr{q}.{tb}")
                A(lambda s, tb=tb: s.activation(
                    out=cwT[:, tb * 128:(tb + 1) * 128],
                    in_=pbt[0][0:NE, 0:128], func=AF.Copy),
                  [K("pb0")], [K_AR("cwT", tb)], f"ctev{q}.{tb}")
            for e in range(NE):
                bank = pb[2 + (e % 2)]
                pkey = K(f"pb{2 + (e % 2)}")
                TE(lambda t, e=e, bank=bank: t.matmul(
                    bank[:], esel[0:8, e * 128:(e + 1) * 128], cwT[:, :],
                    start=True, stop=True),
                   [K("esel")] + [K_AR("cwT", tb) for tb in range(NTB)], [pkey],
                   f"cb{q}.{e}")
                A(lambda s, e=e, bank=bank: s.activation(
                    out=cwb[:, e, :], in_=bank[:], func=AF.Copy),
                  [pkey], [K_AR("cwb", e)], f"cbev{q}.{e}")

            if q == 0:
                tap("cw", cw[:].rearrange("p a b -> p (a b)"),
                    [K_AR("cw", tb) for tb in range(NTB)])
                tap("cwb", cwb.rearrange("p a b -> p (a b)"),
                    [K_AR("cwb", e) for e in range(NE)])

            for e in range(NE + 1):
                for cg in range(2):
                    slot = (e * 2 + cg) % 2
                    wsrc = (h_wu[e] if e < NE else h_w1).rearrange(
                        "(kb p) f -> p kb f", p=128)
                    for hh in range(2):
                        dma("gpsimd", w_u[:, slot, hh * 4:(hh + 1) * 4, :],
                            wsrc[:, hh * 4:(hh + 1) * 4,
                                 cg * 1024:(cg + 1) * 1024],
                            (), [K_AR("w_u", slot)], f"wu{q}.{e}.{cg}.{hh}")
                for jb in range(NFB):
                    cg = jb // 8
                    slot = (e * 2 + cg) % 2
                    bank = pb[2 + (jb % 2)]
                    pkey = K(f"pb{2 + (jb % 2)}")
                    for kb in range(NMB):
                        TE(lambda t, slot=slot, jb=jb, kb=kb, bank=bank:
                           t.matmul(bank[:],
                                    w_u[:, slot, kb,
                                        (jb % 8) * 128:(jb % 8 + 1) * 128],
                                    n_cm[:, kb, :], start=(kb == 0),
                                    stop=(kb == NMB - 1)),
                           [K_AR("w_u", slot), K("n_cm", kb)], [pkey],
                           f"mu{q}.{e}.{jb}.{kb}")
                    A(lambda s, jb=jb, bank=bank: s.activation(
                        out=h2[:, jb, :], in_=bank[:], func=AF.Gelu),
                      [pkey], [K_AR("h2", jb)], f"mug{q}.{e}.{jb}")
                    if e < NE:
                        V(lambda v, e=e, jb=jb: v.tensor_tensor(
                            out=h2[:, jb, :], in0=h2[:, jb, :],
                            in1=cwb[:, e, :], op=ALU.mult),
                          [K_AR("h2", jb), K_AR("cwb", e)], [K_AR("h2", jb)],
                          f"muc{q}.{e}.{jb}")
                wdsrc = (h_wd[e] if e < NE else h_w2).rearrange(
                    "(kb p) d -> p kb d", p=128)

                def wd_dma(g, e=e, wdsrc=wdsrc):
                    dst = w_d[:, (g % 2) * 4:(g % 2) * 4 + 4, :]
                    dma("gpsimd", dst, wdsrc[:, g * 4:(g + 1) * 4, :],
                        (), [K_AR("w_d", g % 2)], f"wd{q}.{e}.{g}")

                wd_dma(0)
                wd_dma(1)
                for kb in range(NFB):
                    if kb == 4:
                        wd_dma(2)
                    if kb == 8:
                        wd_dma(3)
                    wslot = ((kb // 4) % 2) * 4 + (kb % 4)
                    for tb in range(NTB):
                        for dmh in range(2):
                            dst, pkey = op_psum[(tb, dmh)]
                            TE(lambda t, kb=kb, tb=tb, dmh=dmh, dst=dst,
                               wslot=wslot: t.matmul(
                                   dst, h2[:, kb, tb * 128:(tb + 1) * 128],
                                   w_d[:, wslot, dmh * 512:(dmh + 1) * 512],
                                   start=(kb == 0), stop=(kb == NFB - 1)),
                               [K_AR("h2", kb), K_AR("w_d", (kb // 4) % 2)],
                               [pkey], f"md{q}.{e}.{kb}.{tb}.{dmh}")
                for tb in range(NTB):
                    for dmh in range(2):
                        src, pkey = op_psum[(tb, dmh)]
                        if e == 0:
                            A(lambda s, tb=tb, dmh=dmh, src=src: s.activation(
                                out=acc[:, tb, dmh * 512:(dmh + 1) * 512],
                                in_=src, func=AF.Copy),
                              [pkey], [K_AR("acc", tb)],
                              f"mac{q}.{e}.{tb}.{dmh}")
                        else:
                            V(lambda v, tb=tb, dmh=dmh, src=src:
                              v.tensor_tensor(
                                  out=acc[:, tb, dmh * 512:(dmh + 1) * 512],
                                  in0=acc[:, tb, dmh * 512:(dmh + 1) * 512],
                                  in1=src, op=ALU.add),
                              [pkey, K_AR("acc", tb)], [K_AR("acc", tb)],
                              f"mac{q}.{e}.{tb}.{dmh}")

                if q == 0 and e == 0:
                    tap("h2e0", h2.rearrange("p a b -> p (a b)"),
                        [K_AR("h2", jb) for jb in range(NFB)])
                    tap("acc0", acc.rearrange("p a b -> p (a b)"),
                        [K_AR("acc", tb) for tb in range(NTB)])
            if q == 0:
                tap("acc", acc.rearrange("p a b -> p (a b)"),
                    [K_AR("acc", tb) for tb in range(NTB)])

            fence_xbig(f"f_x4{q}")
            ln_stats(acc, lambda tb: K_AR("acc", tb), 68, f"ln4.{q}")
            for tb in range(NTB):
                V(lambda v, tb=tb: v.tensor_scalar(
                    out=scr[:].rearrange("p a b -> p (a b)")[:, 0:D],
                    in0=acc[:, tb, :], scalar1=rows[:, 68 + tb:69 + tb],
                    scalar2=rows[:, 72 + tb:73 + tb],
                    op0=ALU.mult, op1=ALU.add),
                  [K_AR("acc", tb), K("rows", 68 + tb), K("rows", 72 + tb)],
                  [K("scr", 0), K("scr", 1)], f"ln4a{q}.{tb}")
                V(lambda v, tb=tb: v.tensor_tensor(
                    out=x4_f[:, tb, :],
                    in0=scr[:].rearrange("p a b -> p (a b)")[:, 0:D],
                    in1=x2[:, tb, :], op=ALU.add),
                  [K("scr", 0), K("scr", 1), K("x2", tb)],
                  [K("xbig", "x4", tb)], f"ln4b{q}.{tb}")
                dma("sync", h_out[q0 + tb * 128:q0 + (tb + 1) * 128, :],
                    x4_f[:, tb, :],
                    [K("xbig", "x4", tb)], [K("hbm_out", (q, tb))],
                    f"out{q}.{tb}")

        S.add("sync", lambda g: None,
              reads=[K("hbm_out", (q, tb)) for q in range(NQ)
                     for tb in range(NTB)],
              writes=(), tag="final_fence")

        S.finalize(block, sems, dma_sems)
    return nc


_NC_CACHE = {}


def _get_program():
    if "nc" not in _NC_CACHE:
        _NC_CACHE["nc"] = build_program()
    return _NC_CACHE["nc"]


def kernel(L0_attn_x, L0_attn_in_proj_w, L0_attn_conv_w, L0_attn_conv_b,
           L0_attn_x_proj_w, L0_attn_dt_proj_w, L0_attn_dt_proj_b,
           L0_attn_A_log, L0_attn_D, L0_attn_out_proj_w, L0_attn_norm_w,
           L0_ffn_Wg, L0_ffn_Wu, L0_ffn_Wd, L0_ffn_W1, L0_ffn_W2):
    import ml_dtypes
    x = np.ascontiguousarray(np.asarray(L0_attn_x, dtype=np.float32))
    iden_c = np.eye(128, dtype=ml_dtypes.bfloat16)
    esel_c = np.zeros((NST, NST * 128), dtype=ml_dtypes.bfloat16)
    for n in range(NST):
        esel_c[n, n * 128:(n + 1) * 128] = 1.0

    def f32(a):
        return np.ascontiguousarray(np.asarray(a, dtype=np.float32))

    shared = {
        "w_in": f32(L0_attn_in_proj_w),
        "conv_w": f32(L0_attn_conv_w),
        "conv_b": f32(L0_attn_conv_b),
        "x_proj_w": f32(L0_attn_x_proj_w),
        "dt_w": f32(L0_attn_dt_proj_w),
        "dt_b": f32(L0_attn_dt_proj_b),
        "a_log": f32(L0_attn_A_log),
        "d_vec": f32(L0_attn_D),
        "w_out": f32(L0_attn_out_proj_w),
        "norm_w": f32(L0_attn_norm_w),
        "wg": f32(L0_ffn_Wg),
        "wu": f32(L0_ffn_Wu),
        "wd": f32(L0_ffn_Wd),
        "w1": f32(L0_ffn_W1),
        "w2": f32(L0_ffn_W2),
        "c_iden": iden_c,
        "c_esel": esel_c,
    }
    nc = _get_program()
    in_maps = []
    for b in range(B):
        m = dict(shared)
        m["x"] = np.ascontiguousarray(x[b])
        in_maps.append(m)
    res = run_bass_kernel_spmd(nc, in_maps, core_ids=list(range(B)))
    out = np.stack([np.asarray(res.results[b]["out"]) for b in range(B)],
                   axis=0)
    if DBG is not None:
        return out.astype(np.float32), np.asarray(res.results[0]["dbg"])
    return out.astype(np.float32)
